# version 17
# baseline (speedup 1.0000x reference)
import contextlib
import numpy as np
import concourse.bass as bass
import concourse.mybir as mybir
from concourse.bass_utils import run_bass_kernel_spmd

F32 = mybir.dt.float32
BF16 = mybir.dt.bfloat16
AF = mybir.ActivationFunctionType
ALU = mybir.AluOpType
AX = mybir.AxisListType

D = 1024
HEADS = 16
DH = 64
HEXP = 5632
NHC = 44
WGROUPS = [8, 8, 8, 8, 8, 4]
WGOFF = [0, 8, 16, 24, 32, 40]
NEXP = 4
QKSCALE = float(64.0 ** -0.25)
EPS = 1e-5

ENGINES = ("tensor", "vector", "scalar", "gpsimd", "sync")


class Op:
    __slots__ = ("engine", "idx", "fn", "deps", "is_dma", "inc", "inc_idx",
                 "group", "gidx", "tag")

    def __init__(self, engine, idx, fn, is_dma, tag=""):
        self.engine = engine
        self.idx = idx
        self.fn = fn
        self.deps = []
        self.is_dma = is_dma
        self.inc = False
        self.inc_idx = None
        self.group = None
        self.gidx = None
        self.tag = tag


class Buf:
    __slots__ = ("writer", "readers", "name", "n_dma_writes", "gid",
                 "dma_engine", "append")

    def __init__(self, name="", append=False):
        self.writer = None
        self.readers = []
        self.name = name
        self.n_dma_writes = 0
        self.gid = None
        self.dma_engine = None
        self.append = append


class Sched:
    def __init__(self, nc):
        self.nc = nc
        self.ops = {e: [] for e in ENGINES}
        self.groups = []
        self.phase = Buf("phase")

    def op(self, engine, fn, reads=(), writes=(), tag="", is_dma=False,
           _nophase=False):
        rec = Op(engine, len(self.ops[engine]), fn, is_dma, tag)
        deps = set()
        if not _nophase:
            if self.phase.writer is not None:
                deps.add(self.phase.writer)
            self.phase.readers.append(rec)
        for b in reads:
            if b.writer is not None:
                deps.add(b.writer)
            b.readers.append(rec)
        for b in writes:
            if b.writer is not None and not b.append:
                deps.add(b.writer)
            for r in b.readers:
                deps.add(r)
            b.writer = rec
            b.readers = []
        deps.discard(rec)
        rec.deps = deps
        self.ops[engine].append(rec)
        return rec

    def barrier(self):
        def fn(eng):
            return eng.nop()
        return self.op("sync", fn, reads=(), writes=[self.phase],
                       tag="barrier", _nophase=True)

    def dma(self, queue, out, in_, reads=(), writes=(), tag="", **kw):
        assert writes, f"dma op '{tag}' needs a written Buf"
        def fn(eng):
            return eng.dma_start(out, in_, **kw)
        rec = self.op(queue, fn, reads, writes, tag, is_dma=True)
        g = writes[0]
        if g.gid is None:
            g.gid = len(self.groups)
            self.groups.append(g)
        assert g.dma_engine in (None, queue), \
            f"group {g.name} written from two queues"
        g.dma_engine = queue
        g.n_dma_writes += 1
        rec.group = g
        rec.gidx = g.n_dma_writes
        return rec

    def emit(self):
        nc = self.nc
        for e in ENGINES:
            for rec in self.ops[e]:
                for dep in rec.deps:
                    if dep.is_dma:
                        continue
                    if dep.engine == rec.engine and dep.engine == "tensor":
                        continue
                    dep.inc = True
        for e in ENGINES:
            ci = 0
            for rec in self.ops[e]:
                if not rec.is_dma and rec.inc:
                    ci += 1
                    rec.inc_idx = ci

        with contextlib.ExitStack() as stack:
            csem = {e: stack.enter_context(nc.semaphore(f"c_{e}"))
                    for e in ENGINES if self.ops[e]}
            gsem = [stack.enter_context(nc.semaphore(f"g{g.gid}"))
                    for g in self.groups]
            block = stack.enter_context(nc.Block())

            def make_stream(ename):
                recs = self.ops[ename]
                fin = [(g, gsem[g.gid]) for g in self.groups
                       if g.dma_engine == ename]

                def f(eng):
                    maxw = {}
                    for rec in recs:
                        want = {}
                        for dep in rec.deps:
                            if dep.is_dma:
                                key = ("g", dep.group.gid)
                                sem = gsem[dep.group.gid]
                                val = 16 * dep.gidx
                            elif dep.engine == ename and ename == "tensor":
                                continue
                            else:
                                key = ("c", dep.engine)
                                sem = csem[dep.engine]
                                val = dep.inc_idx
                            if key not in want or want[key][1] < val:
                                want[key] = (sem, val)
                        for key in sorted(want):
                            sem, val = want[key]
                            if maxw.get(key, 0) < val:
                                eng.wait_ge(sem, val)
                                maxw[key] = val
                        inst = rec.fn(eng)
                        if rec.is_dma:
                            inst.then_inc(gsem[rec.group.gid], 16)
                        elif rec.inc:
                            inst.then_inc(csem[ename], 1)
                    for g, sem in fin:
                        key = ("g", g.gid)
                        val = 16 * g.n_dma_writes
                        if maxw.get(key, 0) < val:
                            eng.wait_ge(sem, val)
                            maxw[key] = val
                return f

            for ename in ENGINES:
                if self.ops[ename]:
                    getattr(block, ename)(make_stream(ename))


def build(ntg=8):
    S = 512 * ntg
    NKC = 4 * ntg
    nc = bass.Bass("TRN2", target_bir_lowering=False)
    s = Sched(nc)

    x_in = nc.dram_tensor("x", [S, D], F32, kind="ExternalInput")
    wq_d = nc.dram_tensor("wq", [D, D], F32, kind="ExternalInput")
    wk_d = nc.dram_tensor("wk", [D, D], F32, kind="ExternalInput")
    wv_d = nc.dram_tensor("wv", [D, D], F32, kind="ExternalInput")
    wo_d = nc.dram_tensor("wo", [D, D], F32, kind="ExternalInput")
    gam_d = nc.dram_tensor("gamma", [D], F32, kind="ExternalInput")
    bet_d = nc.dram_tensor("beta", [D], F32, kind="ExternalInput")
    gw_d = nc.dram_tensor("gw", [8, D], F32, kind="ExternalInput")
    w1_d = nc.dram_tensor("w1", [NEXP, D, HEXP], F32, kind="ExternalInput")
    w3_d = nc.dram_tensor("w3", [NEXP, D, HEXP], F32, kind="ExternalInput")
    w2_d = nc.dram_tensor("w2", [NEXP, HEXP, D], F32, kind="ExternalInput")
    y_out = nc.dram_tensor("y", [S, D], F32, kind="ExternalOutput")

    qts = nc.dram_tensor("qts", [HEADS, 65, S], BF16)
    kts = nc.dram_tensor("kts", [HEADS, 64, S], BF16)
    x1s = nc.dram_tensor("x1s", [S, D], F32)
    w1s = nc.dram_tensor("w1s", [NEXP, D, HEXP], BF16)
    w3s = nc.dram_tensor("w3s", [NEXP, D, HEXP], BF16)
    w2s = nc.dram_tensor("w2s", [NEXP, HEXP, D], BF16)

    st = contextlib.ExitStack()
    sbufs = {}

    def sb(name, shape, dt, stack=None):
        t = (stack or st).enter_context(nc.sbuf_tensor(name, shape, dt))
        sbufs[name] = t
        return t

    with st:
        xg = sb("xg", [128, 4096], F32)
        ident = sb("ident", [128, 128], F32)
        e16 = sb("e16", [128, 128], BF16)
        gamsb = sb("gamsb", [128, 8], F32)
        betsb = sb("betsb", [128, 8], F32)
        gwT = sb("gwT", [128, 64], F32)
        xc = [sb(f"xc{i}", [128, D], F32) for i in range(2)]
        sqt = [sb(f"sqt{i}", [128, D], F32) for i in range(2)]
        smt = [sb(f"smt{i}", [128, 16], F32) for i in range(2)]
        xnT = sb("xnT", [128, 4096], BF16)

        A = st.enter_context(nc.psum_tensor("A", [128, 1024], F32))
        B = st.enter_context(nc.psum_tensor("B", [128, 1024], F32))
        C = st.enter_context(nc.psum_tensor("C", [128, 512], F32))
        Dp = st.enter_context(nc.psum_tensor("Dp", [128, 512], F32))
        E = st.enter_context(nc.psum_tensor("E", [128, 512], F32))
        F = st.enter_context(nc.psum_tensor("F", [128, 512], F32))

        bxg = [Buf(f"xg{i}") for i in range(4)]
        bident = Buf("ident"); be16 = Buf("e16")
        bgam = Buf("gam"); bbet = Buf("bet"); bgwT = Buf("gwT")
        bxc = [Buf() for _ in range(2)]
        bsqt = [Buf() for _ in range(2)]
        bsmt = [Buf() for _ in range(2)]
        bxnT = Buf("xnT")
        bA0 = Buf("A0"); bA1 = Buf("A1"); bB0 = Buf("B0"); bB1 = Buf("B1")
        bC = Buf("C"); bD = Buf("D"); bE = Buf("E"); bF = Buf("F")

        bqts = Buf("qts", append=True)
        bkts = Buf("kts", append=True)
        bx1s = [Buf(f"x1s{i}") for i in range(ntg)]
        bw1s = [Buf(f"w1s{e}", append=True) for e in range(NEXP)]
        bw3s = [Buf(f"w3s{e}", append=True) for e in range(NEXP)]
        bw2s = [Buf(f"w2s{e}", append=True) for e in range(NEXP)]
        by = [Buf(f"y{i}") for i in range(ntg)]

        ab = contextlib.ExitStack()
        V = sb("V", [128, NKC * 16 * 65], BF16, ab)
        bV = Buf("V")
        V3 = V[:, :].rearrange("p (kc h c) -> p kc h c", h=16, c=65)

        p1 = contextlib.ExitStack()
        wqsb = sb("wqsb", [128, 8192], BF16, p1)
        wksb = sb("wksb", [128, 8192], BF16, p1)
        wvsb = sb("wvsb", [128, 8192], BF16, p1)
        gstage = sb("gstage", [8, 1024], F32, p1)
        qn2 = sb("qn2", [16, S], BF16, p1)
        kn2 = sb("kn2", [16, S], BF16, p1)
        negc = sb("negc", [16, S], BF16, p1)
        qtmp = [sb(f"qtmp{i}", [128, 512], BF16, p1) for i in range(2)]
        gtmp = gstage
        sqtmp = [sb(f"sqtmp{i}", [128, 512], BF16, p1) for i in range(2)]

        bwq = Buf("wq"); bwk = Buf("wk"); bwv = Buf("wv")
        bgstage = Buf("gstage")
        bqn2 = Buf("qn2"); bkn2 = Buf("kn2"); bnegc = Buf("negc")
        bqtmp = [Buf() for _ in range(2)]
        bsqtmp = [Buf() for _ in range(2)]
        bqkst = {"q": [Buf("qst0", append=True), Buf("qst1", append=True)],
                 "k": [Buf("kst0", append=True), Buf("kst1", append=True)]}

        s.op("gpsimd", lambda g: g.memset(ident[:, :], 0.0), writes=[bident])
        s.op("gpsimd", lambda g: g.affine_select(
            out=ident[:, :], in_=ident[:, :], compare_op=ALU.not_equal,
            fill=1.0, base=0, pattern=[[-1, 128]], channel_multiplier=1),
            reads=[bident], writes=[bident])
        s.op("gpsimd", lambda g: g.memset(e16[:, :], 0.0), writes=[be16])
        for c in range(8):
            s.op("gpsimd", lambda g, c=c: g.memset(
                e16[0:64, 16*c + 2*c: 16*c + 2*c + 1], 1.0), writes=[be16])
            s.op("gpsimd", lambda g, c=c: g.memset(
                e16[64:128, 16*c + 2*c + 1: 16*c + 2*c + 2], 1.0),
                writes=[be16])
        s.op("vector", lambda v: v.memset(V[:, :], 1.0), writes=[bV])

        s.dma("scalar", gtmp[0:1, 0:1024],
              gam_d[:].rearrange("(o d) -> o d", o=1),
              reads=[bgstage], writes=[bgstage], tag="gam")
        for dc in range(8):
            s.op("tensor", lambda t, dc=dc: t.transpose(
                E[:, dc:dc+1], gtmp[0:1, 128*dc:128*(dc+1)], ident[0:1, 0:1]),
                reads=[bgstage, bident], writes=[bE])
        s.op("scalar", lambda sc: sc.copy(gamsb[:, :], E[:, 0:8]),
             reads=[bE], writes=[bgam])
        s.dma("scalar", gtmp[0:1, 0:1024],
              bet_d[:].rearrange("(o d) -> o d", o=1),
              reads=[bgstage], writes=[bgstage], tag="bet")
        for dc in range(8):
            s.op("tensor", lambda t, dc=dc: t.transpose(
                F[:, dc:dc+1], gtmp[0:1, 128*dc:128*(dc+1)], ident[0:1, 0:1]),
                reads=[bgstage, bident], writes=[bF])
        s.op("scalar", lambda sc: sc.copy(betsb[:, :], F[:, 0:8]),
             reads=[bF], writes=[bbet])

        s.dma("scalar", gstage[0:8, 0:1024], gw_d[:, :],
              reads=[bgstage], writes=[bgstage], tag="gw")
        for c in range(8):
            s.op("tensor", lambda t, c=c: t.transpose(
                E[:, 8*c:8*c+8], gstage[0:8, 128*c:128*(c+1)],
                ident[0:8, 0:8]),
                reads=[bgstage, bident], writes=[bE])
        s.op("scalar", lambda sc: sc.copy(gwT[:, :], E[:, 0:64]),
             reads=[bE], writes=[bgwT])

        for wd, wsb_, bw in ((wq_d, wqsb, bwq), (wk_d, wksb, bwk),
                             (wv_d, wvsb, bwv)):
            for dc in range(8):
                s.dma("gpsimd", wsb_[:, 1024*dc:1024*(dc+1)],
                      wd[128*dc:128*(dc+1), :],
                      reads=[bw], writes=[bw], tag="wqkv")

        cast_jobs = []
        for e in range(NEXP):
            cast_jobs.append((w1_d, w1s, bw1s, e))
            cast_jobs.append((w3_d, w3s, bw3s, e))
            cast_jobs.append((w2_d, w2s, bw2s, e))
        cast_i = [0]

        def plan_casts(n):
            for _ in range(n):
                if cast_i[0] >= len(cast_jobs):
                    return
                srcd, dstd, bdst, e = cast_jobs[cast_i[0]]
                cast_i[0] += 1
                s.dma("gpsimd", dstd[e][:, :], srcd[e][:, :],
                      writes=[bdst[e]], tag="wcast")

        ncast_per_tg = (len(cast_jobs) + ntg - 1) // ntg

        def plan_ln(r, xten, col0, src_bufs, two=False):
            m = smt[r]

            def one(src_t, c0, sbufs, out_t, oc0, obufs):
                s.op("vector", lambda v: v.bn_stats(
                    m[:, 0:6], src_t[:, c0:c0+512]),
                    reads=sbufs + [bsmt[r]], writes=[bsmt[r]])
                s.op("vector", lambda v: v.bn_stats(
                    m[:, 6:12], src_t[:, c0+512:c0+1024]),
                    reads=sbufs + [bsmt[r]], writes=[bsmt[r]])
                s.op("vector", lambda v: v.bn_aggr(
                    m[:, 12:14], m[:, 0:12]), reads=[bsmt[r]],
                    writes=[bsmt[r]])
                s.op("vector", lambda v: v.tensor_scalar(
                    m[:, 14:15], m[:, 13:14], 1.0, EPS, ALU.mult, ALU.add),
                    reads=[bsmt[r]], writes=[bsmt[r]])
                s.op("scalar", lambda sc: sc.activation(
                    m[:, 15:16], m[:, 14:15], AF.Sqrt),
                    reads=[bsmt[r]], writes=[bsmt[r]])
                s.op("vector", lambda v: v.reciprocal(
                    m[:, 14:15], m[:, 15:16]), reads=[bsmt[r]],
                    writes=[bsmt[r]])
                s.op("vector", lambda v: v.tensor_scalar(
                    out_t[:, oc0:oc0+1024], src_t[:, c0:c0+1024],
                    m[:, 12:13], m[:, 14:15], ALU.subtract, ALU.mult),
                    reads=sbufs + [bsmt[r]], writes=obufs)

            one(xten, col0, src_bufs, xc[r], 0, [bxc[r]])
            if two:
                one(xc[r], 0, [bxc[r]], xc[r], 0, [bxc[r]])

        for tg in range(ntg):
            for ts in range(4):
                s.dma("sync", xg[:, 1024*ts:1024*(ts+1)],
                      x_in[512*tg + 128*ts: 512*tg + 128*(ts+1), :],
                      reads=[bxg[ts]], writes=[bxg[ts]], tag="x1l")
            for ts in range(4):
                r = ts % 2
                plan_ln(r, xg, 1024*ts, [bxg[ts]], two=True)
                for half, bps, pst in ((0, bC, C), (1, bD, Dp)):
                    for q in range(4):
                        dc = 4*half + q
                        s.op("tensor", lambda t, r=r, dc=dc, q=q, pst=pst:
                             t.transpose(pst[:, 128*q:128*(q+1)],
                                         xc[r][:, 128*dc:128*(dc+1)],
                                         ident[:, :]),
                             reads=[bxc[r], bident], writes=[bps])
                    for q in range(4):
                        dc = 4*half + q
                        s.op("scalar", lambda sc, dc=dc, q=q, ts=ts, pst=pst:
                             sc.activation(
                                 xnT[:, 512*dc + 128*ts: 512*dc + 128*(ts+1)],
                                 pst[:, 128*q:128*(q+1)], AF.Identity,
                                 bias=betsb[:, dc:dc+1],
                                 scale=gamsb[:, dc:dc+1]),
                             reads=[bps, bgam, bbet], writes=[bxnT])
            for (wsb_, bw, scr, bscr, n2ps, bn2, n2sb, bn2sb) in (
                    (wqsb, bwq, qts, bqts, A[0:16, 0:512], bA0, qn2, bqn2),
                    (wksb, bwk, kts, bkts, A[0:16, 512:1024], bA1, kn2, bkn2)):
                for mchunk in range(8):
                    sl = mchunk % 2
                    pst, bps = (E, bE) if sl == 0 else (F, bF)
                    for dc in range(8):
                        s.op("tensor", lambda t, wsb_=wsb_, mchunk=mchunk,
                             dc=dc, pst=pst: t.matmul(
                                 pst[:, :],
                                 wsb_[:, 1024*dc + 128*mchunk:
                                      1024*dc + 128*(mchunk+1)],
                                 xnT[:, 512*dc:512*(dc+1)],
                                 start=(dc == 0), stop=(dc == 7)),
                             reads=[bw, bxnT], writes=[bps])
                    s.op("scalar", lambda sc, sl=sl, pst=pst: sc.activation(
                        qtmp[sl][:, :], pst[:, :], AF.Identity, scale=QKSCALE),
                        reads=[bps], writes=[bqtmp[sl]])
                    s.op("scalar", lambda sc, sl=sl, pst=pst: sc.activation(
                        sqtmp[sl][:, :], pst[:, :], AF.Square, scale=QKSCALE),
                        reads=[bps], writes=[bsqtmp[sl]])
                    s.op("tensor", lambda t, sl=sl, mchunk=mchunk, n2ps=n2ps:
                         t.matmul(n2ps, e16[:, 16*mchunk:16*(mchunk+1)],
                                  sqtmp[sl][:, :], start=(mchunk == 0),
                                  stop=(mchunk == 7)),
                         reads=[be16, bsqtmp[sl]], writes=[bn2])
                    stg = bqkst["k" if scr is kts else "q"][sl]
                    for hh in range(2):
                        h = 2*mchunk + hh
                        s.dma("sync", scr[h][0:64, 512*tg:512*(tg+1)],
                              qtmp[sl][64*hh:64*(hh+1), :],
                              reads=[bqtmp[sl]], writes=[stg, bscr],
                              tag="qk_st")
                s.op("scalar", lambda sc, n2ps=n2ps, n2sb=n2sb, tg=tg:
                     sc.copy(n2sb[0:16, 512*tg:512*(tg+1)], n2ps),
                     reads=[bn2], writes=[bn2sb])
            for ts in range(4):
                kc = 4*tg + ts
                for dhh in range(2):
                    sl = dhh % 2
                    pst, bps = (E, bE) if sl == 0 else (F, bF)
                    for dc in range(8):
                        s.op("tensor", lambda t, ts=ts, dhh=dhh, dc=dc, pst=pst:
                             t.matmul(
                                 pst[:, :],
                                 xnT[:, 512*dc + 128*ts: 512*dc + 128*(ts+1)],
                                 wvsb[:, 1024*dc + 512*dhh:
                                      1024*dc + 512*(dhh+1)],
                                 start=(dc == 0), stop=(dc == 7)),
                             reads=[bwv, bxnT], writes=[bps])
                    def vevac(sc, kc=kc, dhh=dhh, pst=pst):
                        o = V3[:, kc, 8*dhh:8*(dhh+1), 0:64]
                        i = pst[:, :].rearrange("p (j c) -> p j c", c=64)
                        return sc.copy(o, i)
                    s.op("scalar", vevac, reads=[bps], writes=[bV])

        s.op("vector", lambda v: v.reduce_max(
            smt[0][0:16, 0:1], kn2[0:16, :], axis=AX.X),
            reads=[bkn2, bsmt[0]], writes=[bsmt[0]])
        s.op("vector", lambda v: v.tensor_scalar(
            negc[0:16, :], qn2[0:16, :], smt[0][0:16, 0:1], -0.5,
            ALU.add, ALU.mult), reads=[bqn2, bsmt[0]], writes=[bnegc])
        s.dma("sync", qts[:, 64, :], negc[0:16, :],
              reads=[bnegc], writes=[Buf("negcst"), bqts], tag="negc")

        p1.close()
        s.barrier()

        p2 = contextlib.ExitStack()
        wosb = sb("wosb", [128, 16384], BF16, p2)
        kbuf = [sb(f"kbuf{i}", [128, S], BF16, p2) for i in range(2)]
        qbuf = [sb(f"qbuf{i}", [128, 512], BF16, p2) for i in range(2)]
        ptil = sb("ptil", [128, NKC * 512], BF16, p2)
        oall = sb("oall", [128, 8192], BF16, p2)
        dsc = xc[0]
        bcsb_t = xc[0]
        ones64 = xc[1]
        bones64 = Buf("ones64")
        s.op("gpsimd", lambda g: g.memset(ones64[64:65, 0:64], 1.0),
             writes=[bones64])

        bwo = Buf("wo")
        bkbuf = [Buf(f"kbuf{i}") for i in range(2)]
        bqbuf = [Buf(f"qbuf{i}") for i in range(2)]
        bptil = [Buf(f"ptil{r}") for r in range(NKC // 2)]
        boall = [Buf(f"oall{h}") for h in range(16)]
        bdsc = Buf("dsc"); bbcsb = Buf("bcsb")

        s.op("gpsimd", lambda g: g.memset(wosb[64:128, :], 0.0),
             writes=[bwo])
        s.op("vector", lambda v: v.memset(oall[64:128, :], 0.0),
             writes=[boall[j] for j in range(16)])
        for h in range(16):
            s.dma("gpsimd", wosb[0:64, 1024*h:1024*(h+1)],
                  wo_d[64*h:64*(h+1), :],
                  reads=[bwo], writes=[bwo], tag="wol")
        for i in range(2):
            s.op("gpsimd", lambda g, i=i: g.memset(kbuf[i][64:128, :], 0.0),
                 writes=[bkbuf[i]])
            s.op("gpsimd", lambda g, i=i: g.memset(kbuf[i][64:65, :], 1.0),
                 reads=[bkbuf[i]], writes=[bkbuf[i]])
            s.op("gpsimd", lambda g, i=i: g.memset(qbuf[i][64:128, :], 0.0),
                 writes=[bqbuf[i]])

        for qg in range(ntg):
            plan_casts(max(2, (len(cast_jobs) + ntg - 1) // ntg))
            for ts in range(4):
                s.dma("sync", xg[:, 1024*ts:1024*(ts+1)],
                      x_in[512*qg + 128*ts: 512*qg + 128*(ts+1), :],
                      reads=[bxg[ts]], writes=[bxg[ts]], tag="x2l")
            pend_div = [None]
            for h in range(16):
                sl = h % 2
                s.dma("sync", kbuf[sl][0:64, :], kts[h][:, :],
                      reads=[bkbuf[sl]], writes=[bkbuf[sl]], tag="kl")
                s.dma("gpsimd", qbuf[sl][0:65, :],
                      qts[h][:, 512*qg:512*(qg+1)],
                      reads=[bqbuf[sl]], writes=[bqbuf[sl]], tag="ql")
                pvp, bpv = (C, bC) if h % 2 == 0 else (Dp, bD)

                def emit_qk(r, sl=sl):
                    P, bp0, bp1 = (A, bA0, bA1) if r % 2 == 0 else (B, bB0, bB1)
                    for half in range(2):
                        kc = 2*r + half
                        s.op("tensor", lambda t, sl=sl, kc=kc, half=half, P=P:
                             t.matmul(P[:, 512*half:512*(half+1)],
                                      kbuf[sl][0:128, 128*kc:128*(kc+1)],
                                      qbuf[sl][0:128, :],
                                      start=True, stop=True),
                             reads=[bkbuf[sl], bqbuf[sl]],
                             writes=[bp0 if half == 0 else bp1])

                def emit_exp(r):
                    P, bp0, bp1 = (A, bA0, bA1) if r % 2 == 0 else (B, bB0, bB1)
                    s.op("scalar", lambda sc, r=r, P=P: sc.activation(
                        ptil[:, 1024*r:1024*(r+1)], P[:, :], AF.Exp),
                        reads=[bp0, bp1], writes=[bptil[r]])

                def emit_pv(r, h=h, pvp=pvp, bpv=bpv):
                    for half in range(2):
                        kc = 2*r + half
                        s.op("tensor", lambda t, kc=kc, h=h, pvp=pvp: t.matmul(
                            pvp[0:65, :], V3[:, kc, h, :],
                            ptil[:, 512*kc:512*(kc+1)],
                            start=(kc == 0), stop=(kc == NKC - 1)),
                            reads=[bV, bptil[r]], writes=[bpv])

                NRP = NKC // 4
                for g in range(NRP):
                    emit_qk(2*g)
                    emit_qk(2*g + 1)
                    if g == 0 and pend_div[0] is not None:
                        pend_div[0]()
                    emit_exp(2*g)
                    emit_exp(2*g + 1)
                    if g >= 1:
                        emit_pv(2*g - 2)
                        emit_pv(2*g - 1)
                emit_pv(NKC//2 - 2)
                emit_pv(NKC//2 - 1)

                def make_div(h=h, sl=sl, pvp=pvp, bpv=bpv):
                    def divchain():
                        s.op("vector", lambda v: v.reciprocal(
                            dsc[64:65, 0:512], pvp[64:65, :]),
                            reads=[bpv], writes=[bdsc])
                        s.op("tensor", lambda t: t.matmul(
                            E[0:64, :], ones64[64:65, 0:64],
                            dsc[64:65, 0:512], start=True, stop=True),
                            reads=[bones64, bdsc], writes=[bE])
                        s.op("scalar", lambda sc: sc.copy(
                            bcsb_t[0:64, 512:1024], E[0:64, :]),
                            reads=[bE], writes=[bbcsb])
                        s.op("vector", lambda v: v.tensor_tensor(
                            oall[0:64, 512*h:512*(h+1)], pvp[0:64, :],
                            bcsb_t[0:64, 512:1024], ALU.mult),
                            reads=[bpv, bbcsb], writes=[boall[h]])
                    return divchain
                pend_div[0] = make_div()
            pend_div[0]()
            for ts in range(4):
                for dhh in range(2):
                    psl = (2*ts + dhh) % 2
                    psA = A[:, 512*psl:512*(psl+1)]
                    bps = bA0 if psl == 0 else bA1
                    for h in range(16):
                        s.op("tensor", lambda t, ts=ts, dhh=dhh, h=h, psA=psA:
                             t.matmul(
                                 psA,
                                 oall[0:128, 512*h + 128*ts: 512*h + 128*(ts+1)],
                                 wosb[0:128, 1024*h + 512*dhh:
                                      1024*h + 512*(dhh+1)],
                                 start=(h == 0), stop=(h == 15)),
                             reads=[boall[h], bwo], writes=[bps])
                    s.op("vector", lambda v, ts=ts, dhh=dhh, psA=psA:
                         v.tensor_tensor(
                             xg[:, 1024*ts + 512*dhh: 1024*ts + 512*(dhh+1)],
                             psA,
                             xg[:, 1024*ts + 512*dhh: 1024*ts + 512*(dhh+1)],
                             ALU.add),
                         reads=[bps, bxg[ts]], writes=[bxg[ts]])
            s.dma("sync",
                  x1s[512*qg:512*(qg+1), :].rearrange(
                      "(ts p) d -> p ts d", p=128),
                  xg[:, :].rearrange("p (ts d) -> p ts d", d=1024),
                  reads=[bxg[0], bxg[1], bxg[2], bxg[3]],
                  writes=[bx1s[qg]], tag="x1st")

        p2.close()
        ab.close()
        s.barrier()

        p3 = contextlib.ExitStack()
        xfT32 = sb("xfT32", [128, 4096], F32, p3)
        w1g = [sb(f"w1g{i}", [128, 8192], BF16, p3) for i in range(2)]
        w3g = [sb(f"w3g{i}", [128, 8192], BF16, p3) for i in range(2)]
        w2g = [sb(f"w2g{i}", [128, 8192], BF16, p3) for i in range(2)]
        s1sb = [sb(f"s1sb{i}", [128, 512], F32, p3) for i in range(2)]
        hcache = [sb(f"hc{i}", [128, 4096], BF16, p3) for i in range(2)]
        yacc = sb("yacc", [128, 4096], F32, p3)
        gsb2 = sb("gsb2", [8, 512], F32, p3)
        glog = sb("glog", [128, 32], F32, p3)
        idxm8 = sb("idxm8", [128, 4], F32, p3)
        gt1 = sb("gt1", [128, 4], F32, p3)
        gt2 = sb("gt2", [128, 4], F32, p3)
        gm = sb("gm", [128, 8], F32, p3)
        exs = sb("exs", [128, 4], F32, p3)
        cwsb = sb("cwsb", [128, 16], F32, p3)

        bxfT32 = Buf("xfT32")
        bw1g = [Buf(f"w1g{i}") for i in range(2)]
        bw3g = [Buf(f"w3g{i}") for i in range(2)]
        bw2g = [Buf(f"w2g{i}") for i in range(2)]
        bs1 = [Buf() for _ in range(2)]
        bhc = [Buf(f"hc{i}") for i in range(2)]
        byacc = [Buf(f"yacc{i}") for i in range(4)]
        bgsb2 = Buf("gsb2"); bglog = Buf("glog")
        bidxm8 = Buf("idxm8")
        bgt1 = Buf(); bgt2 = Buf(); bgm = Buf(); bexs = Buf()
        bcw = Buf("cw")

        for j in range(4):
            s.op("gpsimd", lambda g, j=j: g.memset(
                idxm8[:, j:j+1], float(j) - 8.0), writes=[bidxm8])

        for tg in range(ntg):
            s.dma("sync", xg[:, :].rearrange("p (ts d) -> p ts d", d=1024),
                  x1s[512*tg:512*(tg+1), :].rearrange(
                      "(ts p) d -> p ts d", p=128),
                  reads=[bxg[0], bxg[1], bxg[2], bxg[3]],
                  writes=[bxg[0], bxg[1], bxg[2], bxg[3]], tag="x3l")
            for ts in range(4):
                r = ts % 2
                plan_ln(r, xg, 1024*ts, [bxg[ts]], two=False)
                for half, bps, pst in ((0, bC, C), (1, bD, Dp)):
                    for q in range(4):
                        dc = 4*half + q
                        s.op("tensor", lambda t, r=r, dc=dc, q=q, pst=pst:
                             t.transpose(pst[:, 128*q:128*(q+1)],
                                         xc[r][:, 128*dc:128*(dc+1)],
                                         ident[:, :]),
                             reads=[bxc[r], bident], writes=[bps])
                    def evac32(sc, half=half, ts=ts, pst=pst):
                        o = xfT32[:, :].rearrange("p (dc t) -> p dc t", t=512)
                        return sc.copy(
                            o[:, 4*half:4*half+4, 128*ts:128*(ts+1)],
                            pst[:, :].rearrange("p (q t) -> p q t", t=128))
                    s.op("scalar", evac32, reads=[bps], writes=[bxfT32])
                def evacbf(v, ts=ts):
                    o = xnT[:, :].rearrange("p (dc t) -> p dc t", t=512)
                    i = xfT32[:, :].rearrange("p (dc t) -> p dc t", t=512)
                    return v.tensor_copy(o[:, :, 128*ts:128*(ts+1)],
                                         i[:, :, 128*ts:128*(ts+1)])
                s.op("vector", evacbf, reads=[bxfT32], writes=[bxnT])
                for c in range(8):
                    s.op("tensor", lambda t, c=c, ts=ts: t.matmul(
                        F[0:8, 0:128], gwT[:, 8*c:8*c+8],
                        xfT32[:, 512*c + 128*ts: 512*c + 128*(ts+1)],
                        start=(c == 0), stop=(c == 7)),
                        reads=[bgwT, bxfT32], writes=[bF])
                s.op("scalar", lambda sc, ts=ts: sc.copy(
                    gsb2[0:8, 128*ts:128*(ts+1)], F[0:8, 0:128]),
                    reads=[bF], writes=[bgsb2])
            for ts in range(4):
                s.op("tensor", lambda t, ts=ts: t.transpose(
                    E[:, 8*ts:8*ts+8], gsb2[0:8, 128*ts:128*(ts+1)],
                    ident[0:8, 0:8]), reads=[bgsb2, bident], writes=[bE])
            s.op("scalar", lambda sc: sc.copy(glog[:, :], E[:, 0:32]),
                 reads=[bE], writes=[bglog])
            for ts in range(4):
                lg = glog[:, 8*ts:8*ts+4]
                s.op("vector", lambda v, lg=lg: v.reduce_max(
                    gm[:, 0:1], lg, axis=AX.X), reads=[bglog, bgm],
                    writes=[bgm])
                s.op("vector", lambda v: v.tensor_scalar_mul(
                    gm[:, 1:2], gm[:, 0:1], -1.0), reads=[bgm], writes=[bgm])
                s.op("scalar", lambda sc, lg=lg: sc.activation(
                    exs[:, :], lg, AF.Exp, bias=gm[:, 1:2], scale=1.0),
                    reads=[bglog, bgm], writes=[bexs])
                s.op("vector", lambda v: v.reduce_max(
                    gm[:, 2:3], exs[:, :], axis=AX.X), reads=[bexs, bgm],
                    writes=[bgm])
                s.op("vector", lambda v: v.tensor_scalar(
                    gt1[:, :], exs[:, :], gm[:, 2:3], None, ALU.is_equal),
                    reads=[bexs, bgm], writes=[bgt1])
                s.op("vector", lambda v: v.tensor_tensor(
                    gt1[:, :], gt1[:, :], idxm8[:, :], ALU.mult),
                    reads=[bgt1, bidxm8], writes=[bgt1])
                s.op("vector", lambda v: v.tensor_reduce(
                    gm[:, 3:4], gt1[:, :], axis=AX.X, op=ALU.min),
                    reads=[bgt1, bgm], writes=[bgm])
                s.op("vector", lambda v: v.tensor_scalar(
                    gt1[:, :], idxm8[:, :], gm[:, 3:4], None, ALU.is_equal),
                    reads=[bidxm8, bgm], writes=[bgt1])
                s.op("vector", lambda v: v.scalar_tensor_tensor(
                    gt2[:, :], gt1[:, :], -1e30, exs[:, :], ALU.mult, ALU.add),
                    reads=[bgt1, bexs], writes=[bgt2])
                s.op("vector", lambda v: v.reduce_max(
                    gm[:, 4:5], gt2[:, :], axis=AX.X), reads=[bgt2, bgm],
                    writes=[bgm])
                s.op("vector", lambda v: v.tensor_scalar(
                    gt2[:, :], gt2[:, :], gm[:, 4:5], None, ALU.is_equal),
                    reads=[bgt2, bgm], writes=[bgt2])
                s.op("vector", lambda v: v.tensor_tensor(
                    gt2[:, :], gt2[:, :], idxm8[:, :], ALU.mult),
                    reads=[bgt2, bidxm8], writes=[bgt2])
                s.op("vector", lambda v: v.tensor_reduce(
                    gm[:, 5:6], gt2[:, :], axis=AX.X, op=ALU.min),
                    reads=[bgt2, bgm], writes=[bgm])
                s.op("vector", lambda v: v.tensor_scalar(
                    gt2[:, :], idxm8[:, :], gm[:, 5:6], None, ALU.is_equal),
                    reads=[bidxm8, bgm], writes=[bgt2])
                s.op("vector", lambda v: v.tensor_tensor(
                    gm[:, 6:7], gm[:, 2:3], gm[:, 4:5], ALU.add),
                    reads=[bgm], writes=[bgm])
                s.op("vector", lambda v: v.reciprocal(
                    gm[:, 7:8], gm[:, 6:7]), reads=[bgm], writes=[bgm])
                s.op("vector", lambda v: v.tensor_scalar(
                    gt1[:, :], gt1[:, :], gm[:, 2:3], None, ALU.mult),
                    reads=[bgt1, bgm], writes=[bgt1])
                s.op("vector", lambda v: v.scalar_tensor_tensor(
                    gt1[:, :], gt2[:, :], gm[:, 4:5], gt1[:, :],
                    ALU.mult, ALU.add), reads=[bgt1, bgt2, bgm], writes=[bgt1])
                s.op("vector", lambda v, ts=ts: v.tensor_scalar(
                    cwsb[:, 4*ts:4*ts+4], gt1[:, :], gm[:, 7:8], None,
                    ALU.mult), reads=[bgt1, bgm], writes=[bcw])
            ydone = [0]
            pend_y = [None]
            for e in range(NEXP):
                for hg in range(len(WGROUPS)):
                    gsz = WGROUPS[hg]
                    off = WGOFF[hg]
                    gb = (e * len(WGROUPS) + hg) % 2
                    w1v = w1s[e].rearrange("(dc p) h -> p dc h", p=128)
                    w3v = w3s[e].rearrange("(dc p) h -> p dc h", p=128)
                    w2v = w2s[e].rearrange("(hw p) dd -> p hw dd", p=128)
                    s.dma("gpsimd",
                          w1g[gb][:, 0:1024*gsz].rearrange(
                              "p (dc h) -> p dc h", h=128*gsz),
                          w1v[:, :, 128*off:128*(off+gsz)],
                          reads=[bw1g[gb], bw1s[e]], writes=[bw1g[gb]],
                          tag="w1gl")
                    s.dma("gpsimd",
                          w3g[gb][:, 0:1024*gsz].rearrange(
                              "p (dc h) -> p dc h", h=128*gsz),
                          w3v[:, :, 128*off:128*(off+gsz)],
                          reads=[bw3g[gb], bw3s[e]], writes=[bw3g[gb]],
                          tag="w3gl")
                    s.dma("sync",
                          w2g[gb][:, 0:1024*gsz].rearrange(
                              "p (hw dd) -> p hw dd", dd=1024),
                          w2v[:, off:off+gsz, :],
                          reads=[bw2g[gb], bw2s[e]], writes=[bw2g[gb]],
                          tag="w2gl")
                    for hcw in range(gsz):
                        P, b0, b1 = (A, bA0, bA1) if hcw % 2 == 0 \
                            else (B, bB0, bB1)
                        sl = hcw % 2
                        for dc in range(8):
                            s.op("tensor", lambda t, dc=dc, hcw=hcw, P=P,
                                 gb=gb, gsz=gsz: t.matmul(
                                     P[:, 0:512],
                                     w1g[gb][:, 128*gsz*dc + 128*hcw:
                                             128*gsz*dc + 128*(hcw+1)],
                                     xnT[:, 512*dc:512*(dc+1)],
                                     start=(dc == 0), stop=(dc == 7)),
                                 reads=[bw1g[gb], bxnT], writes=[b0])
                        for dc in range(8):
                            s.op("tensor", lambda t, dc=dc, hcw=hcw, P=P,
                                 gb=gb, gsz=gsz: t.matmul(
                                     P[:, 512:1024],
                                     w3g[gb][:, 128*gsz*dc + 128*hcw:
                                             128*gsz*dc + 128*(hcw+1)],
                                     xnT[:, 512*dc:512*(dc+1)],
                                     start=(dc == 0), stop=(dc == 7)),
                                 reads=[bw3g[gb], bxnT], writes=[b1])
                        s.op("scalar", lambda sc, P=P, sl=sl: sc.activation(
                            s1sb[sl][:, :], P[:, 0:512], AF.Silu),
                            reads=[b0], writes=[bs1[sl]])
                        s.op("vector", lambda v, P=P, sl=sl, gb=gb, hcw=hcw:
                             v.tensor_tensor(
                                 hcache[gb][:, 512*hcw:512*(hcw+1)],
                                 P[:, 512:1024], s1sb[sl][:, :], ALU.mult),
                             reads=[b1, bs1[sl]], writes=[bhc[gb]])
                    if pend_y[0] is not None:
                        pend_y[0]()

                    def make_y(gb=gb, gsz=gsz, hg=hg):
                        def yphase():
                            for ts in range(4):
                                for dhh in range(2):
                                    YP, yb = (C, bC) if ydone[0] % 2 == 0 \
                                        else (Dp, bD)
                                    ydone[0] += 1
                                    for hcw in range(gsz):
                                        s.op("tensor", lambda t, hcw=hcw,
                                             ts=ts, dhh=dhh, YP=YP, gb=gb,
                                             gsz=gsz: t.matmul(
                                                 YP[:, :],
                                                 hcache[gb][:, 512*hcw +
                                                            128*ts:
                                                            512*hcw +
                                                            128*(ts+1)],
                                                 w2g[gb][:, 1024*hcw +
                                                         512*dhh:
                                                         1024*hcw +
                                                         512*(dhh+1)],
                                                 start=(hcw == 0),
                                                 stop=(hcw == gsz - 1)),
                                             reads=[bhc[gb], bw2g[gb]],
                                             writes=[yb])
                                    ysl = yacc[:, 1024*ts + 512*dhh:
                                               1024*ts + 512*(dhh+1)]
                                    if hg == 0:
                                        s.op("vector", lambda v, ysl=ysl,
                                             YP=YP: v.tensor_copy(
                                                 ysl, YP[:, :]),
                                             reads=[yb], writes=[byacc[ts]])
                                    else:
                                        s.op("vector", lambda v, ysl=ysl,
                                             YP=YP: v.tensor_tensor(
                                                 ysl, YP[:, :], ysl, ALU.add),
                                             reads=[yb, byacc[ts]],
                                             writes=[byacc[ts]])
                        return yphase
                    pend_y[0] = make_y()
                pend_y[0]()
                pend_y[0] = None
                for ts in range(4):
                    for dhh in range(2):
                        xsl = xg[:, 1024*ts + 512*dhh: 1024*ts + 512*(dhh+1)]
                        ysl = yacc[:, 1024*ts + 512*dhh:
                                   1024*ts + 512*(dhh+1)]
                        s.op("vector", lambda v, xsl=xsl, ysl=ysl, ts=ts, e=e:
                             v.scalar_tensor_tensor(
                                 xsl, ysl, cwsb[:, 4*ts+e:4*ts+e+1], xsl,
                                 ALU.mult, ALU.add),
                             reads=[byacc[ts], bcw, bxg[ts]],
                             writes=[bxg[ts]])
            s.dma("sync",
                  y_out[512*tg:512*(tg+1), :].rearrange(
                      "(ts p) d -> p ts d", p=128),
                  xg[:, :].rearrange("p (ts d) -> p ts d", d=1024),
                  reads=[bxg[0], bxg[1], bxg[2], bxg[3]],
                  writes=[by[tg]], tag="yst")

        p3.close()
        s.emit()
    return nc


def make_in_map(x_b, inputs):
    return dict(
        x=np.ascontiguousarray(x_b),
        wq=inputs["L0_attn_Wq"], wk=inputs["L0_attn_Wk"],
        wv=inputs["L0_attn_Wv"], wo=inputs["L0_attn_Wo"],
        gamma=inputs["L0_attn_ln_gamma"], beta=inputs["L0_attn_ln_beta"],
        gw=inputs["L0_ffn_gate_w"],
        w1=np.ascontiguousarray(inputs["L0_ffn_w1"][0:NEXP]),
        w3=np.ascontiguousarray(inputs["L0_ffn_w3"][0:NEXP]),
        w2=np.ascontiguousarray(inputs["L0_ffn_w2"][0:NEXP]),
    )


def kernel(**inputs):
    x = np.asarray(inputs["L0_attn_latents"])
    n = x.shape[0]
    nc = build(ntg=8)
    in_maps = [make_in_map(x[c], inputs) for c in range(n)]
    res = run_bass_kernel_spmd(nc, in_maps, core_ids=list(range(n)))
    return np.stack([res.results[c]["y"] for c in range(n)], axis=0)


# version 18
# speedup vs baseline: 1.1656x; 1.1656x over previous
import contextlib
import numpy as np
import concourse.bass as bass
import concourse.mybir as mybir
from concourse.bass_utils import run_bass_kernel_spmd

F32 = mybir.dt.float32
BF16 = mybir.dt.bfloat16
AF = mybir.ActivationFunctionType
ALU = mybir.AluOpType
AX = mybir.AxisListType

D = 1024
HEADS = 16
DH = 64
HEXP = 5632
NHC = 44
WGROUPS = [8, 8, 8, 8, 8, 4]
WGOFF = [0, 8, 16, 24, 32, 40]
NEXP = 4
QKSCALE = float(64.0 ** -0.25)
EPS = 1e-5

ENGINES = ("tensor", "vector", "scalar", "gpsimd", "sync")


class Op:
    __slots__ = ("engine", "idx", "fn", "deps", "is_dma", "inc", "inc_idx",
                 "group", "gidx", "tag")

    def __init__(self, engine, idx, fn, is_dma, tag=""):
        self.engine = engine
        self.idx = idx
        self.fn = fn
        self.deps = []
        self.is_dma = is_dma
        self.inc = False
        self.inc_idx = None
        self.group = None
        self.gidx = None
        self.tag = tag


class Buf:
    __slots__ = ("writer", "readers", "name", "n_dma_writes", "gid",
                 "dma_engine", "append")

    def __init__(self, name="", append=False):
        self.writer = None
        self.readers = []
        self.name = name
        self.n_dma_writes = 0
        self.gid = None
        self.dma_engine = None
        self.append = append


class Sched:
    def __init__(self, nc):
        self.nc = nc
        self.ops = {e: [] for e in ENGINES}
        self.groups = []
        self.phase = Buf("phase")

    def op(self, engine, fn, reads=(), writes=(), tag="", is_dma=False,
           _nophase=False):
        rec = Op(engine, len(self.ops[engine]), fn, is_dma, tag)
        deps = set()
        if not _nophase:
            if self.phase.writer is not None:
                deps.add(self.phase.writer)
            self.phase.readers.append(rec)
        for b in reads:
            if b.writer is not None:
                deps.add(b.writer)
            b.readers.append(rec)
        for b in writes:
            if b.writer is not None and not b.append:
                deps.add(b.writer)
            for r in b.readers:
                deps.add(r)
            b.writer = rec
            b.readers = []
        deps.discard(rec)
        rec.deps = deps
        self.ops[engine].append(rec)
        return rec

    def barrier(self):
        def fn(eng):
            return eng.nop()
        return self.op("sync", fn, reads=(), writes=[self.phase],
                       tag="barrier", _nophase=True)

    def dma(self, queue, out, in_, reads=(), writes=(), tag="", **kw):
        assert writes, f"dma op '{tag}' needs a written Buf"
        def fn(eng):
            return eng.dma_start(out, in_, **kw)
        rec = self.op(queue, fn, reads, writes, tag, is_dma=True)
        g = writes[0]
        if g.gid is None:
            g.gid = len(self.groups)
            self.groups.append(g)
        assert g.dma_engine in (None, queue), \
            f"group {g.name} written from two queues"
        g.dma_engine = queue
        g.n_dma_writes += 1
        rec.group = g
        rec.gidx = g.n_dma_writes
        return rec

    def emit(self):
        nc = self.nc
        for e in ENGINES:
            for rec in self.ops[e]:
                for dep in rec.deps:
                    if dep.is_dma:
                        continue
                    if dep.engine == rec.engine and dep.engine == "tensor":
                        continue
                    dep.inc = True
        for e in ENGINES:
            ci = 0
            for rec in self.ops[e]:
                if not rec.is_dma and rec.inc:
                    ci += 1
                    rec.inc_idx = ci

        with contextlib.ExitStack() as stack:
            csem = {e: stack.enter_context(nc.semaphore(f"c_{e}"))
                    for e in ENGINES if self.ops[e]}
            gsem = [stack.enter_context(nc.semaphore(f"g{g.gid}"))
                    for g in self.groups]
            block = stack.enter_context(nc.Block())

            def make_stream(ename):
                recs = self.ops[ename]
                fin = [(g, gsem[g.gid]) for g in self.groups
                       if g.dma_engine == ename]

                def f(eng):
                    maxw = {}
                    for rec in recs:
                        want = {}
                        for dep in rec.deps:
                            if dep.is_dma:
                                key = ("g", dep.group.gid)
                                sem = gsem[dep.group.gid]
                                val = 16 * dep.gidx
                            elif dep.engine == ename and ename == "tensor":
                                continue
                            else:
                                key = ("c", dep.engine)
                                sem = csem[dep.engine]
                                val = dep.inc_idx
                            if key not in want or want[key][1] < val:
                                want[key] = (sem, val)
                        for key in sorted(want):
                            sem, val = want[key]
                            if maxw.get(key, 0) < val:
                                eng.wait_ge(sem, val)
                                maxw[key] = val
                        inst = rec.fn(eng)
                        if rec.is_dma:
                            inst.then_inc(gsem[rec.group.gid], 16)
                        elif rec.inc:
                            inst.then_inc(csem[ename], 1)
                    for g, sem in fin:
                        key = ("g", g.gid)
                        val = 16 * g.n_dma_writes
                        if maxw.get(key, 0) < val:
                            eng.wait_ge(sem, val)
                            maxw[key] = val
                return f

            for ename in ENGINES:
                if self.ops[ename]:
                    getattr(block, ename)(make_stream(ename))


def build(ntg=8):
    S = 512 * ntg
    NKC = 4 * ntg
    nc = bass.Bass("TRN2", target_bir_lowering=False)
    s = Sched(nc)

    x_in = nc.dram_tensor("x", [S, D], F32, kind="ExternalInput")
    wq_d = nc.dram_tensor("wq", [D, D], F32, kind="ExternalInput")
    wk_d = nc.dram_tensor("wk", [D, D], F32, kind="ExternalInput")
    wv_d = nc.dram_tensor("wv", [D, D], F32, kind="ExternalInput")
    wo_d = nc.dram_tensor("wo", [D, D], F32, kind="ExternalInput")
    gam_d = nc.dram_tensor("gamma", [D], F32, kind="ExternalInput")
    bet_d = nc.dram_tensor("beta", [D], F32, kind="ExternalInput")
    gw_d = nc.dram_tensor("gw", [8, D], F32, kind="ExternalInput")
    w1_d = nc.dram_tensor("w1", [NEXP, D, HEXP], F32, kind="ExternalInput")
    w3_d = nc.dram_tensor("w3", [NEXP, D, HEXP], F32, kind="ExternalInput")
    w2_d = nc.dram_tensor("w2", [NEXP, HEXP, D], F32, kind="ExternalInput")
    y_out = nc.dram_tensor("y", [S, D], F32, kind="ExternalOutput")

    qts = nc.dram_tensor("qts", [HEADS, 65, S], BF16)
    kts = nc.dram_tensor("kts", [HEADS, 64, S], BF16)
    x1s = nc.dram_tensor("x1s", [S, D], F32)
    w1s = nc.dram_tensor("w1s", [NEXP, D, HEXP], BF16)
    w3s = nc.dram_tensor("w3s", [NEXP, D, HEXP], BF16)
    w2s = nc.dram_tensor("w2s", [NEXP, HEXP, D], BF16)

    st = contextlib.ExitStack()
    sbufs = {}

    def sb(name, shape, dt, stack=None):
        t = (stack or st).enter_context(nc.sbuf_tensor(name, shape, dt))
        sbufs[name] = t
        return t

    with st:
        xg = sb("xg", [128, 4096], F32)
        ident = sb("ident", [128, 128], F32)
        e16 = sb("e16", [128, 128], BF16)
        gamsb = sb("gamsb", [128, 8], F32)
        betsb = sb("betsb", [128, 8], F32)
        gwT = sb("gwT", [128, 64], F32)
        xc = [sb(f"xc{i}", [128, D], F32) for i in range(2)]
        sqt = [sb(f"sqt{i}", [128, D], F32) for i in range(2)]
        smt = [sb(f"smt{i}", [128, 16], F32) for i in range(2)]
        xnT = sb("xnT", [128, 4096], BF16)

        A = st.enter_context(nc.psum_tensor("A", [128, 1024], F32))
        B = st.enter_context(nc.psum_tensor("B", [128, 1024], F32))
        C = st.enter_context(nc.psum_tensor("C", [128, 512], F32))
        Dp = st.enter_context(nc.psum_tensor("Dp", [128, 512], F32))
        E = st.enter_context(nc.psum_tensor("E", [128, 512], F32))
        F = st.enter_context(nc.psum_tensor("F", [128, 512], F32))

        bxg = [Buf(f"xg{i}") for i in range(4)]
        bident = Buf("ident"); be16 = Buf("e16")
        bgam = Buf("gam"); bbet = Buf("bet"); bgwT = Buf("gwT")
        bxc = [Buf() for _ in range(2)]
        bsqt = [Buf() for _ in range(2)]
        bsmt = [Buf() for _ in range(2)]
        bxnT = Buf("xnT")
        bA0 = Buf("A0"); bA1 = Buf("A1"); bB0 = Buf("B0"); bB1 = Buf("B1")
        bC = Buf("C"); bD = Buf("D"); bE = Buf("E"); bF = Buf("F")

        bqts = Buf("qts", append=True)
        bkts = Buf("kts", append=True)
        bx1s = [Buf(f"x1s{i}") for i in range(ntg)]
        bw1s = [Buf(f"w1s{e}", append=True) for e in range(NEXP)]
        bw3s = [Buf(f"w3s{e}", append=True) for e in range(NEXP)]
        bw2s = [Buf(f"w2s{e}", append=True) for e in range(NEXP)]
        by = [Buf(f"y{i}") for i in range(ntg)]

        ab = contextlib.ExitStack()
        V = sb("V", [128, NKC * 16 * 65], BF16, ab)
        bV = Buf("V")
        V3 = V[:, :].rearrange("p (kc h c) -> p kc h c", h=16, c=65)

        p1 = contextlib.ExitStack()
        wqsb = sb("wqsb", [128, 8192], BF16, p1)
        wksb = sb("wksb", [128, 8192], BF16, p1)
        wvsb = sb("wvsb", [128, 8192], BF16, p1)
        gstage = sb("gstage", [8, 1024], F32, p1)
        qn2 = sb("qn2", [16, S], BF16, p1)
        kn2 = sb("kn2", [16, S], BF16, p1)
        negc = sb("negc", [16, S], BF16, p1)
        qtmp = [sb(f"qtmp{i}", [128, 512], BF16, p1) for i in range(2)]
        gtmp = gstage
        sqtmp = [sb(f"sqtmp{i}", [128, 512], BF16, p1) for i in range(2)]

        bwq = Buf("wq"); bwk = Buf("wk"); bwv = Buf("wv")
        bgstage = Buf("gstage")
        bqn2 = Buf("qn2"); bkn2 = Buf("kn2"); bnegc = Buf("negc")
        bqtmp = [Buf() for _ in range(2)]
        bsqtmp = [Buf() for _ in range(2)]
        bqkst = {"q": [Buf("qst0", append=True), Buf("qst1", append=True)],
                 "k": [Buf("kst0", append=True), Buf("kst1", append=True)]}

        s.op("gpsimd", lambda g: g.memset(ident[:, :], 0.0), writes=[bident])
        s.op("gpsimd", lambda g: g.affine_select(
            out=ident[:, :], in_=ident[:, :], compare_op=ALU.not_equal,
            fill=1.0, base=0, pattern=[[-1, 128]], channel_multiplier=1),
            reads=[bident], writes=[bident])
        s.op("gpsimd", lambda g: g.memset(e16[:, :], 0.0), writes=[be16])
        for c in range(8):
            s.op("gpsimd", lambda g, c=c: g.memset(
                e16[0:64, 16*c + 2*c: 16*c + 2*c + 1], 1.0), writes=[be16])
            s.op("gpsimd", lambda g, c=c: g.memset(
                e16[64:128, 16*c + 2*c + 1: 16*c + 2*c + 2], 1.0),
                writes=[be16])
        s.op("vector", lambda v: v.memset(V[:, :], 1.0), writes=[bV])

        s.dma("scalar", gtmp[0:1, 0:1024],
              gam_d[:].rearrange("(o d) -> o d", o=1),
              reads=[bgstage], writes=[bgstage], tag="gam")
        for dc in range(8):
            s.op("tensor", lambda t, dc=dc: t.transpose(
                E[:, dc:dc+1], gtmp[0:1, 128*dc:128*(dc+1)], ident[0:1, 0:1]),
                reads=[bgstage, bident], writes=[bE])
        s.op("scalar", lambda sc: sc.copy(gamsb[:, :], E[:, 0:8]),
             reads=[bE], writes=[bgam])
        s.dma("scalar", gtmp[0:1, 0:1024],
              bet_d[:].rearrange("(o d) -> o d", o=1),
              reads=[bgstage], writes=[bgstage], tag="bet")
        for dc in range(8):
            s.op("tensor", lambda t, dc=dc: t.transpose(
                F[:, dc:dc+1], gtmp[0:1, 128*dc:128*(dc+1)], ident[0:1, 0:1]),
                reads=[bgstage, bident], writes=[bF])
        s.op("scalar", lambda sc: sc.copy(betsb[:, :], F[:, 0:8]),
             reads=[bF], writes=[bbet])

        s.dma("scalar", gstage[0:8, 0:1024], gw_d[:, :],
              reads=[bgstage], writes=[bgstage], tag="gw")
        for c in range(8):
            s.op("tensor", lambda t, c=c: t.transpose(
                E[:, 8*c:8*c+8], gstage[0:8, 128*c:128*(c+1)],
                ident[0:8, 0:8]),
                reads=[bgstage, bident], writes=[bE])
        s.op("scalar", lambda sc: sc.copy(gwT[:, :], E[:, 0:64]),
             reads=[bE], writes=[bgwT])

        for wd, wsb_, bw in ((wq_d, wqsb, bwq), (wk_d, wksb, bwk),
                             (wv_d, wvsb, bwv)):
            for dc in range(8):
                s.dma("gpsimd", wsb_[:, 1024*dc:1024*(dc+1)],
                      wd[128*dc:128*(dc+1), :],
                      reads=[bw], writes=[bw], tag="wqkv")

        cast_jobs = []
        for e in range(NEXP):
            cast_jobs.append((w1_d, w1s, bw1s, e))
            cast_jobs.append((w3_d, w3s, bw3s, e))
            cast_jobs.append((w2_d, w2s, bw2s, e))
        cast_i = [0]

        def plan_casts(n):
            for _ in range(n):
                if cast_i[0] >= len(cast_jobs):
                    return
                srcd, dstd, bdst, e = cast_jobs[cast_i[0]]
                cast_i[0] += 1
                s.dma("gpsimd", dstd[e][:, :], srcd[e][:, :],
                      writes=[bdst[e]], tag="wcast")

        ncast_per_tg = (len(cast_jobs) + ntg - 1) // ntg

        def plan_ln(r, xten, col0, src_bufs, two=False):
            m = smt[r]

            def one(src_t, c0, sbufs, out_t, oc0, obufs):
                s.op("vector", lambda v: v.bn_stats(
                    m[:, 0:6], src_t[:, c0:c0+512]),
                    reads=sbufs + [bsmt[r]], writes=[bsmt[r]])
                s.op("vector", lambda v: v.bn_stats(
                    m[:, 6:12], src_t[:, c0+512:c0+1024]),
                    reads=sbufs + [bsmt[r]], writes=[bsmt[r]])
                s.op("vector", lambda v: v.bn_aggr(
                    m[:, 12:14], m[:, 0:12]), reads=[bsmt[r]],
                    writes=[bsmt[r]])
                s.op("vector", lambda v: v.tensor_scalar(
                    m[:, 14:15], m[:, 13:14], 1.0, EPS, ALU.mult, ALU.add),
                    reads=[bsmt[r]], writes=[bsmt[r]])
                s.op("scalar", lambda sc: sc.activation(
                    m[:, 15:16], m[:, 14:15], AF.Sqrt),
                    reads=[bsmt[r]], writes=[bsmt[r]])
                s.op("vector", lambda v: v.reciprocal(
                    m[:, 14:15], m[:, 15:16]), reads=[bsmt[r]],
                    writes=[bsmt[r]])
                s.op("vector", lambda v: v.tensor_scalar(
                    out_t[:, oc0:oc0+1024], src_t[:, c0:c0+1024],
                    m[:, 12:13], m[:, 14:15], ALU.subtract, ALU.mult),
                    reads=sbufs + [bsmt[r]], writes=obufs)

            one(xten, col0, src_bufs, xc[r], 0, [bxc[r]])
            if two:
                one(xc[r], 0, [bxc[r]], xc[r], 0, [bxc[r]])

        for tg in range(ntg):
            for ts in range(4):
                s.dma("sync", xg[:, 1024*ts:1024*(ts+1)],
                      x_in[512*tg + 128*ts: 512*tg + 128*(ts+1), :],
                      reads=[bxg[ts]], writes=[bxg[ts]], tag="x1l")
            for ts in range(4):
                r = ts % 2
                plan_ln(r, xg, 1024*ts, [bxg[ts]], two=True)
                for half, bps, pst in ((0, bC, C), (1, bD, Dp)):
                    for q in range(4):
                        dc = 4*half + q
                        s.op("tensor", lambda t, r=r, dc=dc, q=q, pst=pst:
                             t.transpose(pst[:, 128*q:128*(q+1)],
                                         xc[r][:, 128*dc:128*(dc+1)],
                                         ident[:, :]),
                             reads=[bxc[r], bident], writes=[bps])
                    for q in range(4):
                        dc = 4*half + q
                        s.op("scalar", lambda sc, dc=dc, q=q, ts=ts, pst=pst:
                             sc.activation(
                                 xnT[:, 512*dc + 128*ts: 512*dc + 128*(ts+1)],
                                 pst[:, 128*q:128*(q+1)], AF.Identity,
                                 bias=betsb[:, dc:dc+1],
                                 scale=gamsb[:, dc:dc+1]),
                             reads=[bps, bgam, bbet], writes=[bxnT])
            for (wsb_, bw, scr, bscr, n2ps, bn2, n2sb, bn2sb) in (
                    (wqsb, bwq, qts, bqts, A[0:16, 0:512], bA0, qn2, bqn2),
                    (wksb, bwk, kts, bkts, A[0:16, 512:1024], bA1, kn2, bkn2)):
                for mchunk in range(8):
                    sl = mchunk % 2
                    pst, bps = (E, bE) if sl == 0 else (F, bF)
                    for dc in range(8):
                        s.op("tensor", lambda t, wsb_=wsb_, mchunk=mchunk,
                             dc=dc, pst=pst: t.matmul(
                                 pst[:, :],
                                 wsb_[:, 1024*dc + 128*mchunk:
                                      1024*dc + 128*(mchunk+1)],
                                 xnT[:, 512*dc:512*(dc+1)],
                                 start=(dc == 0), stop=(dc == 7)),
                             reads=[bw, bxnT], writes=[bps])
                    s.op("scalar", lambda sc, sl=sl, pst=pst: sc.activation(
                        qtmp[sl][:, :], pst[:, :], AF.Identity, scale=QKSCALE),
                        reads=[bps], writes=[bqtmp[sl]])
                    s.op("scalar", lambda sc, sl=sl, pst=pst: sc.activation(
                        sqtmp[sl][:, :], pst[:, :], AF.Square, scale=QKSCALE),
                        reads=[bps], writes=[bsqtmp[sl]])
                    s.op("tensor", lambda t, sl=sl, mchunk=mchunk, n2ps=n2ps:
                         t.matmul(n2ps, e16[:, 16*mchunk:16*(mchunk+1)],
                                  sqtmp[sl][:, :], start=(mchunk == 0),
                                  stop=(mchunk == 7)),
                         reads=[be16, bsqtmp[sl]], writes=[bn2])
                    stg = bqkst["k" if scr is kts else "q"][sl]
                    for hh in range(2):
                        h = 2*mchunk + hh
                        s.dma("sync", scr[h][0:64, 512*tg:512*(tg+1)],
                              qtmp[sl][64*hh:64*(hh+1), :],
                              reads=[bqtmp[sl]], writes=[stg, bscr],
                              tag="qk_st")
                s.op("scalar", lambda sc, n2ps=n2ps, n2sb=n2sb, tg=tg:
                     sc.copy(n2sb[0:16, 512*tg:512*(tg+1)], n2ps),
                     reads=[bn2], writes=[bn2sb])
            for ts in range(4):
                kc = 4*tg + ts
                for dhh in range(2):
                    sl = dhh % 2
                    pst, bps = (E, bE) if sl == 0 else (F, bF)
                    for dc in range(8):
                        s.op("tensor", lambda t, ts=ts, dhh=dhh, dc=dc, pst=pst:
                             t.matmul(
                                 pst[:, :],
                                 xnT[:, 512*dc + 128*ts: 512*dc + 128*(ts+1)],
                                 wvsb[:, 1024*dc + 512*dhh:
                                      1024*dc + 512*(dhh+1)],
                                 start=(dc == 0), stop=(dc == 7)),
                             reads=[bwv, bxnT], writes=[bps])
                    def vevac(sc, kc=kc, dhh=dhh, pst=pst):
                        o = V3[:, kc, 8*dhh:8*(dhh+1), 0:64]
                        i = pst[:, :].rearrange("p (j c) -> p j c", c=64)
                        return sc.copy(o, i)
                    s.op("scalar", vevac, reads=[bps], writes=[bV])

        s.op("vector", lambda v: v.reduce_max(
            smt[0][0:16, 0:1], kn2[0:16, :], axis=AX.X),
            reads=[bkn2, bsmt[0]], writes=[bsmt[0]])
        s.op("vector", lambda v: v.tensor_scalar(
            negc[0:16, :], qn2[0:16, :], smt[0][0:16, 0:1], -0.5,
            ALU.add, ALU.mult), reads=[bqn2, bsmt[0]], writes=[bnegc])
        s.dma("sync", qts[:, 64, :], negc[0:16, :],
              reads=[bnegc], writes=[Buf("negcst"), bqts], tag="negc")

        p1.close()
        s.barrier()

        p2 = contextlib.ExitStack()
        wosb = sb("wosb", [128, 16384], BF16, p2)
        kbuf = [sb(f"kbuf{i}", [128, S], BF16, p2) for i in range(2)]
        qbuf = [sb(f"qbuf{i}", [128, 512], BF16, p2) for i in range(2)]
        ptil = sb("ptil", [128, NKC * 512], BF16, p2)
        oall = sb("oall", [128, 8192], BF16, p2)
        dsc = xc[0]
        bcsb_t = xc[0]
        ones64 = xc[1]
        bones64 = Buf("ones64")
        s.op("gpsimd", lambda g: g.memset(ones64[64:65, 0:64], 1.0),
             writes=[bones64])

        bwo = Buf("wo")
        bkbuf = [Buf(f"kbuf{i}") for i in range(2)]
        bqbuf = [Buf(f"qbuf{i}") for i in range(2)]
        bptil = [Buf(f"ptil{r}") for r in range(NKC // 2)]
        boall = [Buf(f"oall{h}") for h in range(16)]
        bdsc = Buf("dsc"); bbcsb = Buf("bcsb")

        s.op("gpsimd", lambda g: g.memset(wosb[64:128, :], 0.0),
             writes=[bwo])
        s.op("vector", lambda v: v.memset(oall[64:128, :], 0.0),
             writes=[boall[j] for j in range(16)])
        for h in range(16):
            s.dma("gpsimd", wosb[0:64, 1024*h:1024*(h+1)],
                  wo_d[64*h:64*(h+1), :],
                  reads=[bwo], writes=[bwo], tag="wol")
        for i in range(2):
            s.op("gpsimd", lambda g, i=i: g.memset(kbuf[i][64:128, :], 0.0),
                 writes=[bkbuf[i]])
            s.op("gpsimd", lambda g, i=i: g.memset(kbuf[i][64:65, :], 1.0),
                 reads=[bkbuf[i]], writes=[bkbuf[i]])
            s.op("gpsimd", lambda g, i=i: g.memset(qbuf[i][64:128, :], 0.0),
                 writes=[bqbuf[i]])

        for qg in range(ntg):
            plan_casts(max(2, (len(cast_jobs) + ntg - 1) // ntg))
            for ts in range(4):
                s.dma("sync", xg[:, 1024*ts:1024*(ts+1)],
                      x_in[512*qg + 128*ts: 512*qg + 128*(ts+1), :],
                      reads=[bxg[ts]], writes=[bxg[ts]], tag="x2l")
            pend_div = [None]
            for h in range(16):
                sl = h % 2
                s.dma("sync", kbuf[sl][0:64, :], kts[h][:, :],
                      reads=[bkbuf[sl]], writes=[bkbuf[sl]], tag="kl")
                s.dma("gpsimd", qbuf[sl][0:65, :],
                      qts[h][:, 512*qg:512*(qg+1)],
                      reads=[bqbuf[sl]], writes=[bqbuf[sl]], tag="ql")
                pvp, bpv = C, bC

                def qslot(r, half):
                    sel = r % 3
                    if sel == 0:
                        return A[:, 512*half:512*(half+1)], (bA0, bA1)[half]
                    if sel == 1:
                        return B[:, 512*half:512*(half+1)], (bB0, bB1)[half]
                    return ((Dp, E)[half])[:, :], (bD, bE)[half]

                def emit_qk(r, sl=sl):
                    for half in range(2):
                        kc = 2*r + half
                        pap, pbuf = qslot(r, half)
                        s.op("tensor", lambda t, sl=sl, kc=kc, pap=pap:
                             t.matmul(pap,
                                      kbuf[sl][0:128, 128*kc:128*(kc+1)],
                                      qbuf[sl][0:128, :],
                                      start=True, stop=True),
                             reads=[bkbuf[sl], bqbuf[sl]],
                             writes=[pbuf])

                def emit_exp(r):
                    if r % 3 == 2:
                        for half in range(2):
                            pap, pbuf = qslot(r, half)
                            s.op("scalar", lambda sc, r=r, half=half,
                                 pap=pap, pbuf=pbuf: sc.activation(
                                     ptil[:, 1024*r + 512*half:
                                          1024*r + 512*(half+1)],
                                     pap, AF.Exp),
                                 reads=[pbuf], writes=[bptil[r]])
                    else:
                        P = A if r % 3 == 0 else B
                        bp0, bp1 = (bA0, bA1) if r % 3 == 0 else (bB0, bB1)
                        s.op("scalar", lambda sc, r=r, P=P: sc.activation(
                            ptil[:, 1024*r:1024*(r+1)], P[:, :], AF.Exp),
                            reads=[bp0, bp1], writes=[bptil[r]])

                def emit_pv(r, h=h, pvp=pvp, bpv=bpv):
                    for half in range(2):
                        kc = 2*r + half
                        s.op("tensor", lambda t, kc=kc, h=h, pvp=pvp: t.matmul(
                            pvp[0:65, :], V3[:, kc, h, :],
                            ptil[:, 512*kc:512*(kc+1)],
                            start=(kc == 0), stop=(kc == NKC - 1)),
                            reads=[bV, bptil[r]], writes=[bpv])

                NRP = NKC // 4
                for g in range(NRP):
                    emit_qk(2*g)
                    emit_qk(2*g + 1)
                    if g == 0 and pend_div[0] is not None:
                        pend_div[0]()
                    emit_exp(2*g)
                    emit_exp(2*g + 1)
                    if g >= 1:
                        emit_pv(2*g - 2)
                        emit_pv(2*g - 1)
                emit_pv(NKC//2 - 2)
                emit_pv(NKC//2 - 1)

                def make_div(h=h, sl=sl, pvp=pvp, bpv=bpv):
                    def divchain():
                        s.op("vector", lambda v: v.reciprocal(
                            dsc[64:65, 0:512], pvp[64:65, :]),
                            reads=[bpv], writes=[bdsc])
                        s.op("tensor", lambda t: t.matmul(
                            F[0:64, :], ones64[64:65, 0:64],
                            dsc[64:65, 0:512], start=True, stop=True),
                            reads=[bones64, bdsc], writes=[bF])
                        s.op("scalar", lambda sc: sc.copy(
                            bcsb_t[0:64, 512:1024], F[0:64, :]),
                            reads=[bF], writes=[bbcsb])
                        s.op("vector", lambda v: v.tensor_tensor(
                            oall[0:64, 512*h:512*(h+1)], pvp[0:64, :],
                            bcsb_t[0:64, 512:1024], ALU.mult),
                            reads=[bpv, bbcsb], writes=[boall[h]])
                    return divchain
                pend_div[0] = make_div()
            pend_div[0]()
            for ts in range(4):
                for dhh in range(2):
                    psl = (2*ts + dhh) % 2
                    psA = A[:, 512*psl:512*(psl+1)]
                    bps = bA0 if psl == 0 else bA1
                    for h in range(16):
                        s.op("tensor", lambda t, ts=ts, dhh=dhh, h=h, psA=psA:
                             t.matmul(
                                 psA,
                                 oall[0:128, 512*h + 128*ts: 512*h + 128*(ts+1)],
                                 wosb[0:128, 1024*h + 512*dhh:
                                      1024*h + 512*(dhh+1)],
                                 start=(h == 0), stop=(h == 15)),
                             reads=[boall[h], bwo], writes=[bps])
                    s.op("vector", lambda v, ts=ts, dhh=dhh, psA=psA:
                         v.tensor_tensor(
                             xg[:, 1024*ts + 512*dhh: 1024*ts + 512*(dhh+1)],
                             psA,
                             xg[:, 1024*ts + 512*dhh: 1024*ts + 512*(dhh+1)],
                             ALU.add),
                         reads=[bps, bxg[ts]], writes=[bxg[ts]])
            s.dma("sync",
                  x1s[512*qg:512*(qg+1), :].rearrange(
                      "(ts p) d -> p ts d", p=128),
                  xg[:, :].rearrange("p (ts d) -> p ts d", d=1024),
                  reads=[bxg[0], bxg[1], bxg[2], bxg[3]],
                  writes=[bx1s[qg]], tag="x1st")

        p2.close()
        ab.close()
        s.barrier()

        p3 = contextlib.ExitStack()
        xfT32 = sb("xfT32", [128, 4096], F32, p3)
        w1g = [sb(f"w1g{i}", [128, 8192], BF16, p3) for i in range(2)]
        w3g = [sb(f"w3g{i}", [128, 8192], BF16, p3) for i in range(2)]
        w2g = [sb(f"w2g{i}", [128, 8192], BF16, p3) for i in range(2)]
        s1sb = [sb(f"s1sb{i}", [128, 512], F32, p3) for i in range(2)]
        hcache = [sb(f"hc{i}", [128, 4096], BF16, p3) for i in range(2)]
        yacc = sb("yacc", [128, 4096], F32, p3)
        gsb2 = sb("gsb2", [8, 512], F32, p3)
        glog = sb("glog", [128, 32], F32, p3)
        idxm8 = sb("idxm8", [128, 4], F32, p3)
        gt1 = sb("gt1", [128, 4], F32, p3)
        gt2 = sb("gt2", [128, 4], F32, p3)
        gm = sb("gm", [128, 8], F32, p3)
        exs = sb("exs", [128, 4], F32, p3)
        cwsb = sb("cwsb", [128, 16], F32, p3)

        bxfT32 = Buf("xfT32")
        bw1g = [Buf(f"w1g{i}") for i in range(2)]
        bw3g = [Buf(f"w3g{i}") for i in range(2)]
        bw2g = [Buf(f"w2g{i}") for i in range(2)]
        bs1 = [Buf() for _ in range(2)]
        bhc = [Buf(f"hc{i}") for i in range(2)]
        byacc = [Buf(f"yacc{i}") for i in range(4)]
        bgsb2 = Buf("gsb2"); bglog = Buf("glog")
        bidxm8 = Buf("idxm8")
        bgt1 = Buf(); bgt2 = Buf(); bgm = Buf(); bexs = Buf()
        bcw = Buf("cw")

        for j in range(4):
            s.op("gpsimd", lambda g, j=j: g.memset(
                idxm8[:, j:j+1], float(j) - 8.0), writes=[bidxm8])

        for tg in range(ntg):
            s.dma("sync", xg[:, :].rearrange("p (ts d) -> p ts d", d=1024),
                  x1s[512*tg:512*(tg+1), :].rearrange(
                      "(ts p) d -> p ts d", p=128),
                  reads=[bxg[0], bxg[1], bxg[2], bxg[3]],
                  writes=[bxg[0], bxg[1], bxg[2], bxg[3]], tag="x3l")
            for ts in range(4):
                r = ts % 2
                plan_ln(r, xg, 1024*ts, [bxg[ts]], two=False)
                for half, bps, pst in ((0, bC, C), (1, bD, Dp)):
                    for q in range(4):
                        dc = 4*half + q
                        s.op("tensor", lambda t, r=r, dc=dc, q=q, pst=pst:
                             t.transpose(pst[:, 128*q:128*(q+1)],
                                         xc[r][:, 128*dc:128*(dc+1)],
                                         ident[:, :]),
                             reads=[bxc[r], bident], writes=[bps])
                    def evac32(sc, half=half, ts=ts, pst=pst):
                        o = xfT32[:, :].rearrange("p (dc t) -> p dc t", t=512)
                        return sc.copy(
                            o[:, 4*half:4*half+4, 128*ts:128*(ts+1)],
                            pst[:, :].rearrange("p (q t) -> p q t", t=128))
                    s.op("scalar", evac32, reads=[bps], writes=[bxfT32])
                def evacbf(v, ts=ts):
                    o = xnT[:, :].rearrange("p (dc t) -> p dc t", t=512)
                    i = xfT32[:, :].rearrange("p (dc t) -> p dc t", t=512)
                    return v.tensor_copy(o[:, :, 128*ts:128*(ts+1)],
                                         i[:, :, 128*ts:128*(ts+1)])
                s.op("vector", evacbf, reads=[bxfT32], writes=[bxnT])
                for c in range(8):
                    s.op("tensor", lambda t, c=c, ts=ts: t.matmul(
                        F[0:8, 0:128], gwT[:, 8*c:8*c+8],
                        xfT32[:, 512*c + 128*ts: 512*c + 128*(ts+1)],
                        start=(c == 0), stop=(c == 7)),
                        reads=[bgwT, bxfT32], writes=[bF])
                s.op("scalar", lambda sc, ts=ts: sc.copy(
                    gsb2[0:8, 128*ts:128*(ts+1)], F[0:8, 0:128]),
                    reads=[bF], writes=[bgsb2])
            for ts in range(4):
                s.op("tensor", lambda t, ts=ts: t.transpose(
                    E[:, 8*ts:8*ts+8], gsb2[0:8, 128*ts:128*(ts+1)],
                    ident[0:8, 0:8]), reads=[bgsb2, bident], writes=[bE])
            s.op("scalar", lambda sc: sc.copy(glog[:, :], E[:, 0:32]),
                 reads=[bE], writes=[bglog])
            for ts in range(4):
                lg = glog[:, 8*ts:8*ts+4]
                s.op("vector", lambda v, lg=lg: v.reduce_max(
                    gm[:, 0:1], lg, axis=AX.X), reads=[bglog, bgm],
                    writes=[bgm])
                s.op("vector", lambda v: v.tensor_scalar_mul(
                    gm[:, 1:2], gm[:, 0:1], -1.0), reads=[bgm], writes=[bgm])
                s.op("scalar", lambda sc, lg=lg: sc.activation(
                    exs[:, :], lg, AF.Exp, bias=gm[:, 1:2], scale=1.0),
                    reads=[bglog, bgm], writes=[bexs])
                s.op("vector", lambda v: v.reduce_max(
                    gm[:, 2:3], exs[:, :], axis=AX.X), reads=[bexs, bgm],
                    writes=[bgm])
                s.op("vector", lambda v: v.tensor_scalar(
                    gt1[:, :], exs[:, :], gm[:, 2:3], None, ALU.is_equal),
                    reads=[bexs, bgm], writes=[bgt1])
                s.op("vector", lambda v: v.tensor_tensor(
                    gt1[:, :], gt1[:, :], idxm8[:, :], ALU.mult),
                    reads=[bgt1, bidxm8], writes=[bgt1])
                s.op("vector", lambda v: v.tensor_reduce(
                    gm[:, 3:4], gt1[:, :], axis=AX.X, op=ALU.min),
                    reads=[bgt1, bgm], writes=[bgm])
                s.op("vector", lambda v: v.tensor_scalar(
                    gt1[:, :], idxm8[:, :], gm[:, 3:4], None, ALU.is_equal),
                    reads=[bidxm8, bgm], writes=[bgt1])
                s.op("vector", lambda v: v.scalar_tensor_tensor(
                    gt2[:, :], gt1[:, :], -1e30, exs[:, :], ALU.mult, ALU.add),
                    reads=[bgt1, bexs], writes=[bgt2])
                s.op("vector", lambda v: v.reduce_max(
                    gm[:, 4:5], gt2[:, :], axis=AX.X), reads=[bgt2, bgm],
                    writes=[bgm])
                s.op("vector", lambda v: v.tensor_scalar(
                    gt2[:, :], gt2[:, :], gm[:, 4:5], None, ALU.is_equal),
                    reads=[bgt2, bgm], writes=[bgt2])
                s.op("vector", lambda v: v.tensor_tensor(
                    gt2[:, :], gt2[:, :], idxm8[:, :], ALU.mult),
                    reads=[bgt2, bidxm8], writes=[bgt2])
                s.op("vector", lambda v: v.tensor_reduce(
                    gm[:, 5:6], gt2[:, :], axis=AX.X, op=ALU.min),
                    reads=[bgt2, bgm], writes=[bgm])
                s.op("vector", lambda v: v.tensor_scalar(
                    gt2[:, :], idxm8[:, :], gm[:, 5:6], None, ALU.is_equal),
                    reads=[bidxm8, bgm], writes=[bgt2])
                s.op("vector", lambda v: v.tensor_tensor(
                    gm[:, 6:7], gm[:, 2:3], gm[:, 4:5], ALU.add),
                    reads=[bgm], writes=[bgm])
                s.op("vector", lambda v: v.reciprocal(
                    gm[:, 7:8], gm[:, 6:7]), reads=[bgm], writes=[bgm])
                s.op("vector", lambda v: v.tensor_scalar(
                    gt1[:, :], gt1[:, :], gm[:, 2:3], None, ALU.mult),
                    reads=[bgt1, bgm], writes=[bgt1])
                s.op("vector", lambda v: v.scalar_tensor_tensor(
                    gt1[:, :], gt2[:, :], gm[:, 4:5], gt1[:, :],
                    ALU.mult, ALU.add), reads=[bgt1, bgt2, bgm], writes=[bgt1])
                s.op("vector", lambda v, ts=ts: v.tensor_scalar(
                    cwsb[:, 4*ts:4*ts+4], gt1[:, :], gm[:, 7:8], None,
                    ALU.mult), reads=[bgt1, bgm], writes=[bcw])
            ydone = [0]
            pend_y = [None]
            for e in range(NEXP):
                for hg in range(len(WGROUPS)):
                    gsz = WGROUPS[hg]
                    off = WGOFF[hg]
                    gb = (e * len(WGROUPS) + hg) % 2
                    w1v = w1s[e].rearrange("(dc p) h -> p dc h", p=128)
                    w3v = w3s[e].rearrange("(dc p) h -> p dc h", p=128)
                    w2v = w2s[e].rearrange("(hw p) dd -> p hw dd", p=128)
                    s.dma("gpsimd",
                          w1g[gb][:, 0:1024*gsz].rearrange(
                              "p (dc h) -> p dc h", h=128*gsz),
                          w1v[:, :, 128*off:128*(off+gsz)],
                          reads=[bw1g[gb], bw1s[e]], writes=[bw1g[gb]],
                          tag="w1gl")
                    s.dma("gpsimd",
                          w3g[gb][:, 0:1024*gsz].rearrange(
                              "p (dc h) -> p dc h", h=128*gsz),
                          w3v[:, :, 128*off:128*(off+gsz)],
                          reads=[bw3g[gb], bw3s[e]], writes=[bw3g[gb]],
                          tag="w3gl")
                    s.dma("sync",
                          w2g[gb][:, 0:1024*gsz].rearrange(
                              "p (hw dd) -> p hw dd", dd=1024),
                          w2v[:, off:off+gsz, :],
                          reads=[bw2g[gb], bw2s[e]], writes=[bw2g[gb]],
                          tag="w2gl")
                    for hcw in range(gsz):
                        P, b0, b1 = (A, bA0, bA1) if hcw % 2 == 0 \
                            else (B, bB0, bB1)
                        sl = hcw % 2
                        for dc in range(8):
                            s.op("tensor", lambda t, dc=dc, hcw=hcw, P=P,
                                 gb=gb, gsz=gsz: t.matmul(
                                     P[:, 0:512],
                                     w1g[gb][:, 128*gsz*dc + 128*hcw:
                                             128*gsz*dc + 128*(hcw+1)],
                                     xnT[:, 512*dc:512*(dc+1)],
                                     start=(dc == 0), stop=(dc == 7)),
                                 reads=[bw1g[gb], bxnT], writes=[b0])
                        for dc in range(8):
                            s.op("tensor", lambda t, dc=dc, hcw=hcw, P=P,
                                 gb=gb, gsz=gsz: t.matmul(
                                     P[:, 512:1024],
                                     w3g[gb][:, 128*gsz*dc + 128*hcw:
                                             128*gsz*dc + 128*(hcw+1)],
                                     xnT[:, 512*dc:512*(dc+1)],
                                     start=(dc == 0), stop=(dc == 7)),
                                 reads=[bw3g[gb], bxnT], writes=[b1])
                        s.op("scalar", lambda sc, P=P, sl=sl: sc.activation(
                            s1sb[sl][:, :], P[:, 0:512], AF.Silu),
                            reads=[b0], writes=[bs1[sl]])
                        s.op("vector", lambda v, P=P, sl=sl, gb=gb, hcw=hcw:
                             v.tensor_tensor(
                                 hcache[gb][:, 512*hcw:512*(hcw+1)],
                                 P[:, 512:1024], s1sb[sl][:, :], ALU.mult),
                             reads=[b1, bs1[sl]], writes=[bhc[gb]])
                    if pend_y[0] is not None:
                        pend_y[0]()

                    def make_y(gb=gb, gsz=gsz, hg=hg):
                        def yphase():
                            for ts in range(4):
                                for dhh in range(2):
                                    YP, yb = (C, bC) if ydone[0] % 2 == 0 \
                                        else (Dp, bD)
                                    ydone[0] += 1
                                    for hcw in range(gsz):
                                        s.op("tensor", lambda t, hcw=hcw,
                                             ts=ts, dhh=dhh, YP=YP, gb=gb,
                                             gsz=gsz: t.matmul(
                                                 YP[:, :],
                                                 hcache[gb][:, 512*hcw +
                                                            128*ts:
                                                            512*hcw +
                                                            128*(ts+1)],
                                                 w2g[gb][:, 1024*hcw +
                                                         512*dhh:
                                                         1024*hcw +
                                                         512*(dhh+1)],
                                                 start=(hcw == 0),
                                                 stop=(hcw == gsz - 1)),
                                             reads=[bhc[gb], bw2g[gb]],
                                             writes=[yb])
                                    ysl = yacc[:, 1024*ts + 512*dhh:
                                               1024*ts + 512*(dhh+1)]
                                    if hg == 0:
                                        s.op("vector", lambda v, ysl=ysl,
                                             YP=YP: v.tensor_copy(
                                                 ysl, YP[:, :]),
                                             reads=[yb], writes=[byacc[ts]])
                                    else:
                                        s.op("vector", lambda v, ysl=ysl,
                                             YP=YP: v.tensor_tensor(
                                                 ysl, YP[:, :], ysl, ALU.add),
                                             reads=[yb, byacc[ts]],
                                             writes=[byacc[ts]])
                        return yphase
                    pend_y[0] = make_y()
                pend_y[0]()
                pend_y[0] = None
                for ts in range(4):
                    for dhh in range(2):
                        xsl = xg[:, 1024*ts + 512*dhh: 1024*ts + 512*(dhh+1)]
                        ysl = yacc[:, 1024*ts + 512*dhh:
                                   1024*ts + 512*(dhh+1)]
                        s.op("vector", lambda v, xsl=xsl, ysl=ysl, ts=ts, e=e:
                             v.scalar_tensor_tensor(
                                 xsl, ysl, cwsb[:, 4*ts+e:4*ts+e+1], xsl,
                                 ALU.mult, ALU.add),
                             reads=[byacc[ts], bcw, bxg[ts]],
                             writes=[bxg[ts]])
            s.dma("sync",
                  y_out[512*tg:512*(tg+1), :].rearrange(
                      "(ts p) d -> p ts d", p=128),
                  xg[:, :].rearrange("p (ts d) -> p ts d", d=1024),
                  reads=[bxg[0], bxg[1], bxg[2], bxg[3]],
                  writes=[by[tg]], tag="yst")

        p3.close()
        s.emit()
    return nc


def make_in_map(x_b, inputs):
    return dict(
        x=np.ascontiguousarray(x_b),
        wq=inputs["L0_attn_Wq"], wk=inputs["L0_attn_Wk"],
        wv=inputs["L0_attn_Wv"], wo=inputs["L0_attn_Wo"],
        gamma=inputs["L0_attn_ln_gamma"], beta=inputs["L0_attn_ln_beta"],
        gw=inputs["L0_ffn_gate_w"],
        w1=np.ascontiguousarray(inputs["L0_ffn_w1"][0:NEXP]),
        w3=np.ascontiguousarray(inputs["L0_ffn_w3"][0:NEXP]),
        w2=np.ascontiguousarray(inputs["L0_ffn_w2"][0:NEXP]),
    )


def kernel(**inputs):
    x = np.asarray(inputs["L0_attn_latents"])
    n = x.shape[0]
    nc = build(ntg=8)
    in_maps = [make_in_map(x[c], inputs) for c in range(n)]
    res = run_bass_kernel_spmd(nc, in_maps, core_ids=list(range(n)))
    return np.stack([res.results[c]["y"] for c in range(n)], axis=0)
